# version 09
# speedup vs baseline: 1.1150x; 1.0158x over previous
import numpy as np
import ml_dtypes
from contextlib import ExitStack

import concourse.bass as bass
import concourse.mybir as mybir
from concourse.bass_utils import run_bass_kernel_spmd

F32 = mybir.dt.float32
BF16 = mybir.dt.bfloat16
I8 = mybir.dt.int8
AF = mybir.ActivationFunctionType
ALU = mybir.AluOpType
AX = mybir.AxisListType

T = 2048
D = 1024
DC = D // 128
HF = 4096
HC = HF // 128
NQ = 4
QT = T // NQ
NT = T // 128
KV = 512
KVC = KV // 128
NH = 16
HD = 64
NE = 8

BUILD_UPTO = 6
MM_LINES = None
DBG = None


class SemC:
    __slots__ = ("h", "c")
    def __init__(self, h):
        self.h = h
        self.c = 0


class EM:
    ENGS = ("tensor", "vector", "scalar", "gpsimd", "sync")

    def __init__(self, nc, ctx):
        self.nc = nc
        self.ctx = ctx
        self.q = {e: [] for e in self.ENGS}
        self.self_sems = {}

    def sem(self, name):
        return SemC(self.ctx.enter_context(self.nc.semaphore(name)))

    def emit(self, eng, fn, wait=(), inc=None, inc_n=1):
        wait = [t for t in wait if t is not None]
        ss = self.self_sems.get(eng)
        if ss is not None:
            wait = [(ss, ss.c)] + wait
            if inc is None:
                inc = ss
        w = [(s.h, int(v)) for (s, v) in wait if v > 0]
        if inc is not None:
            inc.c += inc_n
            i = (inc.h, inc_n)
        else:
            i = None

        def run(engine, w=w, i=i, fn=fn):
            for h, v in w:
                engine.wait_ge(h, v)
            ins = fn(engine)
            if i is not None:
                ins.then_inc(*i)
        self.q[eng].append(run)

    def wait_only(self, eng, wait):
        w = [(s.h, int(v)) for (s, v) in (t for t in wait if t is not None) if v > 0]

        def run(engine, w=w):
            for h, v in w:
                engine.wait_ge(h, v)
        self.q[eng].append(run)

    def finish(self):
        with self.nc.Block() as block:
            for e in self.ENGS:
                fns = self.q[e]
                if not fns:
                    continue
                def section(engine, fns=fns):
                    with self.nc.allow_low_precision(
                            reason="intentional bf16 operand layout"):
                        for f in fns:
                            f(engine)
                getattr(block, e)(section)


def build_program():
    nc = bass.Bass("TRN2", target_bir_lowering=False, dynamic_dma_scratch_size=16384)

    x_in = nc.dram_tensor("x", [D, T], F32, kind="ExternalInput")
    wqkvT = nc.dram_tensor("wqkvT", [D, 3 * D], F32, kind="ExternalInput")
    woT = nc.dram_tensor("woT", [D, D], F32, kind="ExternalInput")
    bo_in = nc.dram_tensor("bo", [128, DC], F32, kind="ExternalInput")
    w1h = nc.dram_tensor("w1h", [128, HC, DC, 128], F32, kind="ExternalInput")
    w2h = nc.dram_tensor("w2h", [128, HC, DC, 128], F32, kind="ExternalInput")
    cw_in = nc.dram_tensor("cw", [128, HC * 3], F32, kind="ExternalInput")
    cb_in = nc.dram_tensor("cb", [128, HC], F32, kind="ExternalInput")
    wqT1 = nc.dram_tensor("wqT1", [D, KV], F32, kind="ExternalInput")
    wkT1 = nc.dram_tensor("wkT1", [D, KV], F32, kind="ExternalInput")
    wvT1 = nc.dram_tensor("wvT1", [D, KV], F32, kind="ExternalInput")
    woT1_in = nc.dram_tensor("woT1", [KV, D], F32, kind="ExternalInput")
    gwT = nc.dram_tensor("gwT", [D, NE], F32, kind="ExternalInput")
    e0T = nc.dram_tensor("e0T", [D, D], F32, kind="ExternalInput")
    e1T = nc.dram_tensor("e1T", [D, D], F32, kind="ExternalInput")
    owT = nc.dram_tensor("owT", [D, D], F32, kind="ExternalInput")
    identbf_in = nc.dram_tensor("ident_bf", [128, 128], BF16,
                                kind="ExternalInput")
    identf_in = nc.dram_tensor("ident_f", [128, 128], F32,
                               kind="ExternalInput")
    onesbf_in = nc.dram_tensor("ones_bf", [128, 1], BF16,
                               kind="ExternalInput")
    onesf_in = nc.dram_tensor("ones_f", [128, 1], F32, kind="ExternalInput")
    mask_in = nc.dram_tensor("mask_su", [128, 128], BF16,
                             kind="ExternalInput")

    out_t = nc.dram_tensor("out", [D, T], F32, kind="ExternalOutput")
    dbg_t = nc.dram_tensor("dbg", [128, 2 * DC * T], F32,
                           kind="ExternalOutput") if DBG else None

    wq1s = nc.dram_tensor("wq1s", [HC, 128, DC * 128], I8)
    wq2s = nc.dram_tensor("wq2s", [HC, 128, DC * 128], I8)
    NROW = 8
    rowbuf = nc.dram_tensor("rowbuf", [NROW, T], F32)

    uidc = [0]
    def uid(base):
        uidc[0] += 1
        return f"{base}_{uidc[0]}"

    ctx = ExitStack()
    with ctx:
        em = EM(nc, ctx)
        sb = lambda n, s, d: ctx.enter_context(nc.sbuf_tensor(n, s, d))

        x_sb = sb("x_sb", [128, DC * T], F32)
        ident_bf = sb("identbf", [128, 128], BF16)
        ident_f = sb("identf", [128, 128], F32)
        ones_bf = sb("onesbf", [128, 1], BF16)
        ones_f = sb("onesf", [128, 1], F32)
        mask_su = sb("masksu", [128, 128], BF16)
        q1_col = sb("q1col", [128, 1], F32)
        q2_col = sb("q2col", [128, 1], F32)
        cw_sb = sb("cwsb", [128, HC * 3], F32)
        cw2_sb = sb("cw2sb", [128, HC * 3], F32)
        cw2_done = [False]
        cb_sb = sb("cbsb", [128, HC], F32)
        bo_sb = sb("bosb", [128, DC], F32)
        trtmp = sb("trtmp", [32, 256], F32)
        tok_a = sb("tok_a", [128, 64], F32)
        tok_b = sb("tok_b", [128, 64], F32)
        scol = sb("scol", [128, 1], F32)

        Spe = em.sem("Spe")
        Sdv = em.sem("Sdv")
        Ssc = em.sem("Ssc")
        Sgp = em.sem("Sgp")
        Sst = em.sem("Sst")
        em.self_sems = {"vector": Sdv, "scalar": Ssc, "gpsimd": Sgp}

        rowslot = {"i": 0, "tags": [None] * NROW, "g1": None}

        class Chain:
            def __init__(self, name, eng):
                self.sem = em.sem(name)
                self.eng = eng

        CHX = Chain("chx", "sync")
        CST = Chain("cst", "sync")
        CR = Chain("crow", "sync")
        CW = [Chain("cw0", "sync"), Chain("cw1", "sync")]
        CQ = [Chain("cq0", "gpsimd"), Chain("cq1", "gpsimd")]
        CG = [Chain("cg0", "gpsimd"), Chain("cg1", "gpsimd")]
        CDN = Chain("cdn", "gpsimd")
        wrot = [0]

        def wchain():
            wrot[0] += 1
            return CG[wrot[0] % 2]

        def dma(ch, dst, src, wait=()):
            sm = ch.sem
            w = [(sm, sm.c)] + [t for t in wait if t is not None]
            em.emit(ch.eng, lambda e, d=dst, s=src: e.dma_start(d, s),
                    wait=w, inc=sm, inc_n=16)
            return (sm, sm.c)

        def mm(out_ap, lhsT_ap, rhs_ap, start, stop, wait=(), inc=None,
               tile_position=None):
            if MM_LINES is not None:
                import sys
                MM_LINES.append(sys._getframe(1).f_lineno)
            em.emit("tensor",
                    lambda e, o=out_ap, l=lhsT_ap, r=rhs_ap, a=start, b=stop,
                    tp=tile_position: e.matmul(o, l, r, start=a, stop=b,
                                               tile_position=tp),
                    wait=wait, inc=inc)

        def tt(eng, out_ap, a_ap, b_ap, op, wait=(), inc=None):
            em.emit(eng, lambda e, o=out_ap, a=a_ap, b=b_ap, p=op:
                    e.tensor_tensor(o, a, b, p), wait=wait, inc=inc)

        def tsc(eng, out_ap, in_ap, s1, s2, op0, op1=None, wait=(), inc=None):
            if op1 is None:
                em.emit(eng, lambda e, o=out_ap, i=in_ap, a=s1, p=op0:
                        e.tensor_scalar(o, i, a, None, p), wait=wait, inc=inc)
            else:
                em.emit(eng, lambda e, o=out_ap, i=in_ap, a=s1, b=s2, p=op0,
                        q=op1: e.tensor_scalar(o, i, a, b, p, q),
                        wait=wait, inc=inc)

        def stt(eng, out_ap, in0_ap, scal, in1_ap, op0, op1, wait=(),
                inc=None):
            em.emit(eng, lambda e, o=out_ap, i=in0_ap, s=scal, j=in1_ap,
                    p=op0, q=op1: e.scalar_tensor_tensor(o, i, s, j, p, q),
                    wait=wait, inc=inc)

        def cp(eng, out_ap, in_ap, wait=(), inc=None):
            if eng == "scalar":
                em.emit(eng, lambda e, o=out_ap, i=in_ap: e.copy(o, i),
                        wait=wait, inc=inc)
            else:
                em.emit(eng, lambda e, o=out_ap, i=in_ap: e.tensor_copy(o, i),
                        wait=wait, inc=inc)

        def red(eng, out_ap, in_ap, op, absval=False, wait=(), inc=None):
            em.emit(eng, lambda e, o=out_ap, i=in_ap, p=op, a=absval:
                    e.tensor_reduce(o, i, axis=AX.X, op=p,
                                    apply_absolute_value=a),
                    wait=wait, inc=inc)

        def recip(eng, out_ap, in_ap, wait=(), inc=None):
            em.emit(eng, lambda e, o=out_ap, i=in_ap: e.reciprocal(o, i),
                    wait=wait, inc=inc)

        def act(out_ap, in_ap, func, bias=0.0, scale=1.0, accum=None,
                wait=(), inc=None):
            em.emit("scalar", lambda e, o=out_ap, i=in_ap, f=func, b=bias,
                    s=scale, a=accum: e.activation(o, i, f, bias=b, scale=s,
                                                   accum_out=a),
                    wait=wait, inc=inc)

        def transpose(out_ap, in_ap, ident, wait=(), inc=None):
            n = in_ap.shape[0]
            if MM_LINES is not None:
                import sys
                MM_LINES.append(sys._getframe(1).f_lineno)
            em.emit("tensor", lambda e, o=out_ap, i=in_ap, d=ident[0:n, 0:n]:
                    e.transpose(o, i, d), wait=wait, inc=inc)

        def memset(eng, ap, val, wait=(), inc=None):
            em.emit(eng, lambda e, a=ap, v=val: e.memset(a, v),
                    wait=wait, inc=inc)

        ALL_CHAINS = [CHX, CST, CR, CDN] + CW + CQ + CG

        def guard():
            return [(Sdv, Sdv.c), (Ssc, Ssc.c), (Sgp, Sgp.c), (Spe, Spe.c)] + \
                   [(c.sem, c.sem.c) for c in ALL_CHAINS]

        def bcast_tok(tok_ap, k, bc_ap, pre_wait=()):
            slot = rowslot["i"] % NROW
            rowslot["i"] += 1
            with nc.psum_tensor(uid("bcps"), [32, 512], F32) as bps:
                transpose(bps[0:k, 0:128], tok_ap, ident_f,
                          wait=list(pre_wait) + [(Sdv, Sdv.c)], inc=Spe)
                cp("vector", trtmp[0:k, 0:128], bps[0:k, 0:128],
                   wait=[(Spe, Spe.c)] +
                        ([rowslot["g1"]] if rowslot["g1"] else []), inc=Sdv)
            old = rowslot["tags"][slot]
            g1 = dma(CR, rowbuf.ap()[slot:slot + 1, 0:128 * k].rearrange(
                         "q (p m) -> q p m", p=k),
                     trtmp[0:k, 0:128], wait=[(Sdv, Sdv.c), old])
            rowslot["g1"] = g1
            g2 = dma(CR, bc_ap,
                     bass.AP(rowbuf, slot * T, [[0, 128], [1, 128 * k]]),
                     wait=[g1])
            rowslot["tags"][slot] = g2
            return g2

        for dc in range(DC):
            dma(CHX, x_sb[:, dc * T:(dc + 1) * T],
                x_in.ap()[dc * 128:(dc + 1) * 128, :])
        dma(CHX, ident_bf[:, :], identbf_in.ap())
        dma(CHX, ident_f[:, :], identf_in.ap())
        dma(CHX, ones_bf[:, :], onesbf_in.ap())
        dma(CHX, ones_f[:, :], onesf_in.ap())
        dma(CHX, mask_su[:, :], mask_in.ap())
        dma(CHX, cw_sb[:, :], cw_in.ap())
        dma(CHX, cb_sb[:, :], cb_in.ap())
        const_tag = dma(CHX, bo_sb[:, :], bo_in.ap())
        for e in ("vector", "scalar", "tensor", "gpsimd"):
            em.wait_only(e, [const_tag])

        def prep_weight(wh, wqs, q_col, name, half, lazy):
            pctx = ExitStack()
            SW = 4 * 128 if half else DC * 128
            NS = HC * 2 if half else HC
            H = {}

            def s_alloc():
                psb = lambda n, s, d: pctx.enter_context(
                    nc.sbuf_tensor(uid(n), s, d))
                H["slab"] = [psb(f"{name}sl", [128, SW], F32)
                             for _ in range(2)]
                H["part"] = psb(f"{name}pp", [128, NS + 1], F32)
                H["g8"] = [psb(f"{name}g8", [128, SW], I8) for _ in range(2)]
                H["prow"] = psb(f"{name}pr", [1, 192], F32)
                H["thp"] = psb(f"{name}th", [128, 2], F32)
            cwl = [Chain(uid(f"{name}cw"), "gpsimd") for _ in range(2)]
            cq = [Chain(uid(f"{name}cq"), "gpsimd") for _ in range(2)]
            ALL_CHAINS.extend(cwl + cq)
            ltags = {}
            cons = {}
            gcons = {}
            qtags = []
            st = {}
            steps = []

            def src(k):
                if half:
                    hc, hf = k // 2, k % 2
                    return wh.ap()[:, hc, hf * 4:(hf + 1) * 4, :] \
                        .rearrange("p a b -> p (a b)")
                return wh.ap()[:, k, :, :].rearrange("p a b -> p (a b)")

            def load(key):
                b = key % 2
                ltags[key] = dma(cwl[b], H["slab"][b][:, :], src(key % NS),
                                 wait=[cons.get(b)] +
                                      (guard() if key < 2 and lazy else []))

            def s_p1(k):
                if k == 0:
                    load(0)
                    load(1)
                act(H["slab"][k % 2][:, :], H["slab"][k % 2][:, :],
                    AF.Abs, accum=H["part"][:, k:k + 1],
                    wait=[ltags[k]], inc=Ssc)
                cons[k % 2] = (Ssc, Ssc.c)
                if k + 2 < NS:
                    load(k + 2)

            def s_fin():
                red("vector", H["part"][:, NS:NS + 1],
                    H["part"][:, 0:NS], ALU.add,
                    wait=[(Ssc, Ssc.c)], inc=Sdv)
                slot = rowslot["i"] % NROW
                rowslot["i"] += 1
                old = rowslot["tags"][slot]
                g1 = dma(cwl[0], rowbuf.ap()[slot:slot + 1, 0:128]
                         .rearrange("q (p m) -> q p m", p=128),
                         H["part"][:, NS:NS + 1],
                         wait=[(Sdv, Sdv.c), old])
                g2 = dma(cwl[0], H["prow"][0:1, 0:128],
                         rowbuf.ap()[slot:slot + 1, 0:128], wait=[g1])
                red("vector", H["prow"][0:1, 128:129],
                    H["prow"][0:1, 0:128], ALU.add, wait=[g2], inc=Sdv)
                tsc("vector", H["prow"][0:1, 129:130],
                    H["prow"][0:1, 128:129],
                    1.0 / (HF * D), 1e-5, ALU.mult, ALU.max, inc=Sdv)
                tsc("vector", H["prow"][0:1, 130:131],
                    H["prow"][0:1, 129:130], 0.5, None, ALU.mult, inc=Sdv)
                tsc("vector", H["prow"][0:1, 131:132],
                    H["prow"][0:1, 129:130], -0.5, None, ALU.mult, inc=Sdv)
                g3 = dma(cwl[0], rowbuf.ap()[slot:slot + 1, 128:131],
                         H["prow"][0:1, 129:132], wait=[(Sdv, Sdv.c)])
                st["q"] = dma(cwl[0], q_col[:, 0:1],
                              bass.AP(rowbuf, slot * T + 128,
                                      [[0, 128], [1, 1]]), wait=[g3])
                dma(cwl[0], H["thp"][:, 0:1],
                    bass.AP(rowbuf, slot * T + 129, [[0, 128], [1, 1]]))
                st["th"] = dma(cwl[0], H["thp"][:, 1:2],
                               bass.AP(rowbuf, slot * T + 130,
                                       [[0, 128], [1, 1]]))
                rowslot["tags"][slot] = st["th"]

            def s_p2(k):
                b = k % 2
                if k == 0:
                    load(NS)
                    load(NS + 1)
                w0 = [ltags[NS + k], gcons.get(b)]
                if k < 2:
                    w0.append(st["th"])
                tsc("vector", H["g8"][b][:, :], H["slab"][b][:, :],
                    H["thp"][:, 0:1], None, ALU.is_ge, wait=w0, inc=Sdv)
                tsc("vector", H["slab"][b][:, :], H["slab"][b][:, :],
                    H["thp"][:, 1:2], None, ALU.is_le, inc=Sdv)
                tt("vector", H["g8"][b][:, :], H["g8"][b][:, :],
                   H["slab"][b][:, :], ALU.subtract, inc=Sdv)
                cons[b] = (Sdv, Sdv.c)
                if half:
                    hc, hf = k // 2, k % 2
                    wdst = wqs.ap()[hc, :, hf * 512:(hf + 1) * 512]
                else:
                    wdst = wqs.ap()[k, :, :]
                gcons[b] = dma(cq[b], wdst, H["g8"][b][:, :],
                               wait=[(Sdv, Sdv.c)])
                qtags.append(gcons[b])
                if k + 2 < NS:
                    load(NS + k + 2)

            def s_end():
                st["done"] = [qtags[-2], qtags[-1],
                              (cwl[0].sem, cwl[0].sem.c),
                              (cwl[1].sem, cwl[1].sem.c)]

            st["pctx"] = pctx
            st["cq"] = cq
            st["qw"] = (cwl[0].sem, 16 * (NS // 2 + 4))
            if lazy:
                steps.append(s_alloc)
            else:
                s_alloc()
            for k in range(NS):
                steps.append(lambda k=k: s_p1(k))
            steps.append(s_fin)
            for k in range(NS):
                steps.append(lambda k=k: s_p2(k))
            steps.append(s_end)
            return qtags, st, steps

        w2q, w2st, w2steps = prep_weight(w2h, wq2s, q2_col, "w2",
                                         True, False)
        w1q, w1st, w1steps = prep_weight(w1h, wq1s, q1_col, "w1",
                                         True, False)

        def prep_pre():
            c1 = w1steps[:2 * HC]
            del w1steps[:2 * HC]
            c2 = w2steps[:2 * HC]
            del w2steps[:2 * HC]
            while c1 or c2:
                if c1:
                    c1.pop(0)()
                if c2:
                    c2.pop(0)()

        def prep_drain(n=2):
            for _ in range(n):
                if w1steps:
                    w1steps.pop(0)()
                elif w2steps:
                    w2steps.pop(0)()

        def emit_ln(xln, keep=None):
            with ExitStack() as lctx:
                lsb = lambda n, s, d: lctx.enter_context(
                    nc.sbuf_tensor(uid(n), s, d))
                scr = lsb("ln_scr", [128, 2 * T], BF16)
                inplace = xln.dtype == F32
                r_bc = lsb("ln_rbc", [128, T], F32)
                mr_bc = lsb("ln_mrbc", [128, T], F32)
                u_tmp = mr_bc if inplace else lsb("ln_u", [128, T], F32)
                with ExitStack() as pctx:
                    sum_ps = pctx.enter_context(
                        nc.psum_tensor(uid("ln_ps1"), [1, T], F32))
                    sq_ps = pctx.enter_context(
                        nc.psum_tensor(uid("ln_ps2"), [1, T], F32))
                    g0 = guard()
                    for dc in range(DC):
                        cp("vector", scr[:, 0:T], x_sb[:, dc * T:(dc + 1) * T],
                           wait=(g0 if dc == 0 else [(Spe, Spe.c)]), inc=Sdv)
                        act(scr[:, T:2 * T], x_sb[:, dc * T:(dc + 1) * T],
                            AF.Square,
                            wait=(g0 if dc == 0 else [(Spe, Spe.c)]),
                            inc=Ssc)
                        for w in range(T // 512):
                            mm(sum_ps[0:1, w * 512:(w + 1) * 512],
                               ones_bf[:, :], scr[:, w * 512:(w + 1) * 512],
                               start=(dc == 0), stop=(dc == DC - 1),
                               wait=([(Sdv, Sdv.c)] +
                                     (g0 if dc == 0 else []))
                               if w == 0 else (), inc=Spe)
                            mm(sq_ps[0:1, w * 512:(w + 1) * 512],
                               ones_bf[:, :],
                               scr[:, T + w * 512:T + (w + 1) * 512],
                               start=(dc == 0), stop=(dc == DC - 1),
                               wait=[(Ssc, Ssc.c)] if w == 0 else (), inc=Spe)
                    cp("vector", u_tmp[0:1, 0:T], sum_ps[0:1, :],
                       wait=[(Spe, Spe.c)], inc=Sdv)
                    cp("vector", r_bc[0:1, 0:T], sq_ps[0:1, :], inc=Sdv)
                with nc.psum_tensor(uid("ln_tp"), [128, 512], F32) as stT:
                    for i in range(NT):
                        transpose(stT[:, 2 * i:2 * i + 1],
                                  u_tmp[0:1, i * 128:(i + 1) * 128],
                                  ident_f,
                                  wait=[(Sdv, Sdv.c)] if i == 0 else (),
                                  inc=Spe)
                        transpose(stT[:, 2 * i + 1:2 * i + 2],
                                  r_bc[0:1, i * 128:(i + 1) * 128],
                                  ident_f, inc=Spe)
                    cp("vector", tok_a[:, 0:32], stT[:, 0:32],
                       wait=[(Spe, Spe.c)], inc=Sdv)
                sum_v = bass.AP(tok_a, 0, [[64, 128], [2, NT]])
                sq_v = bass.AP(tok_a, 1, [[64, 128], [2, NT]])
                m_tok = tok_a[:, 32:48]
                r_tok = tok_a[:, 48:64]
                v_tok = tok_b[:, 0:16]
                mr_tok = tok_b[:, 16:32]
                keep_dv = Sdv.c
                tsc("vector", m_tok, sum_v, 1.0 / D, None, ALU.mult, inc=Sdv)
                tsc("vector", v_tok, sq_v, 1.0 / D, None, ALU.mult, inc=Sdv)
                tt("vector", r_tok, m_tok, m_tok, ALU.mult, inc=Sdv)
                tt("vector", v_tok, v_tok, r_tok, ALU.subtract, inc=Sdv)
                tsc("vector", v_tok, v_tok, 1e-5, None, ALU.add, inc=Sdv)
                recip("vector", r_tok, v_tok, inc=Sdv)
                act(r_tok, r_tok, AF.Sqrt, wait=[(Sdv, Sdv.c)], inc=Ssc)
                tt("vector", mr_tok, m_tok, r_tok, ALU.mult,
                   wait=[(Ssc, Ssc.c)], inc=Sdv)
                if keep is not None:
                    cp("vector", keep[:, 0:16], r_tok, inc=Sdv)
                    cp("vector", keep[:, 16:32], m_tok, inc=Sdv)
                    keep_dv = Sdv.c
                bcast_tok(r_tok, NT, r_bc[:, :])
                b2 = bcast_tok(mr_tok, NT, mr_bc[:, :])
                xtags = []
                for dc in range(DC):
                    u = xln[:, dc * T:(dc + 1) * T] if inplace \
                        else u_tmp[:, :]
                    tt("vector", u, x_sb[:, dc * T:(dc + 1) * T],
                       r_bc[:, :], ALU.mult,
                       wait=[b2] if dc == 0 else (), inc=Sdv)
                    tt("vector", xln[:, dc * T:(dc + 1) * T], u,
                       mr_bc[:, :], ALU.subtract, inc=Sdv)
                    xtags.append((Sdv, Sdv.c))
            return {"done": (Sdv, Sdv.c), "xln": xtags, "b2": b2,
                    "keep": keep_dv}

        def ffn_l0(pre=None, drain=None):
            with ExitStack() as fctx:
                fsb = lambda n, s, d: fctx.enter_context(
                    nc.sbuf_tensor(uid(n), s, d))
                xsc1 = fsb("xsc1", [128, DC * T], BF16)
                slg = [fsb("slg", [128, 2 * DC * 128], BF16)
                       for _ in range(2)]
                callfence = guard()
                xsc_tag = []
                with ExitStack() as qctx:
                    qsb = lambda n, s, d: qctx.enter_context(
                        nc.sbuf_tensor(uid(n), s, d))
                    xln = qsb("xlnf", [128, DC * T], F32)
                    emit_ln(xln)
                    xi8 = qsb("xi8q", [128, T], I8)
                    s_bc = qsb("s1bc", [128, T], F32)
                    sv_bc = qsb("sv1bc", [128, T], F32)
                    amaxp = s_bc
                    red("vector", amaxp[:, :],
                        bass.AP(xln, 0, [[DC * T, 128], [1, T], [T, DC]]),
                        ALU.max, absval=True, inc=Sdv)
                    with nc.psum_tensor(uid("amps"), [128, 512], F32) as am_ps:
                        for i in range(NT):
                            transpose(am_ps[:, 0:128],
                                      amaxp[:, i * 128:(i + 1) * 128],
                                      ident_f[:, :], wait=[(Sdv, Sdv.c)],
                                      inc=Spe)
                            red("vector", tok_b[:, 32 + i:33 + i],
                                am_ps[:, 0:128], ALU.max,
                                wait=[(Spe, Spe.c)], inc=Sdv)
                    amax_tok = tok_b[:, 32:48]
                    s_tok = tok_b[:, 0:16]
                    sv_tok = tok_b[:, 16:32]
                    tsc("vector", amax_tok, amax_tok, 1e-5, None, ALU.max,
                        inc=Sdv)
                    recip("vector", s_tok, amax_tok, inc=Sdv)
                    tsc("vector", s_tok, s_tok, 127.0, None, ALU.mult,
                        inc=Sdv)
                    if pre:
                        pre()
                    tsc("vector", sv_tok, amax_tok, 1.0 / 127.0, None,
                        ALU.mult, inc=Sdv)
                    bcast_tok(s_tok, NT, s_bc[:, :])
                    b2 = bcast_tok(sv_tok, NT, sv_bc[:, :])
                    q0_fence = Sdv.c
                    for dc in range(DC):
                        tt("vector", xi8[:, :], xln[:, dc * T:(dc + 1) * T],
                           s_bc[:, :], ALU.mult,
                           wait=[b2] if dc == 0 else (), inc=Sdv)
                        tt("vector", xsc1[:, dc * T:(dc + 1) * T], xi8[:, :],
                           sv_bc[:, :], ALU.mult, inc=Sdv)
                        xsc_tag.append((Sdv, Sdv.c))
                    if DBG == "xsc1":
                        dma(CDN, dbg_t.ap()[:, 0:DC * T], xsc1[:, :],
                            wait=[(Sdv, Sdv.c)])
                        dma(CDN, dbg_t.ap()[:, DC * T:2 * DC * T],
                            xln[:, :], wait=[(Sdv, Sdv.c)])
                ysilu = fsb("ysilu", [128, HC * QT], F32)
                atmp = fsb("atmp", [128, QT], F32)
                t1tmp = fsb("t1tmp", [128, QT], F32)
                yconv = fsb("yconv", [128, QT], F32)
                xi2 = fsb("xi2", [128, QT], I8)
                Hl = [fsb("hlsv", [128, HC], F32) for _ in range(2)]
                xsc2 = [fsb("xsc2", [128, QT], BF16) for _ in range(2)]
                s2_bc = fsb("s2bc", [128, QT], F32)
                sv2_bc = fsb("sv2bc", [128, QT], F32)
                amax2p = fsb("am2p", [128, QT], F32)
                slab_cons = [Spe.c, Spe.c]
                memset("vector", Hl[0][:, :], 0.0, wait=guard(), inc=Sdv)
                for q in range(NQ):
                    qoff = q * QT
                    with ExitStack() as hctx:
                        H_ps = [hctx.enter_context(nc.psum_tensor(
                            uid("Hps"), [128, 1024], F32)) for _ in range(2)]
                        sc_done = [0, 0]
                        dv_done = [Sdv.c, Sdv.c]
                        gtags = [None] * 16
                        def load1(g):
                            c = 16 * (2 * g + 2)
                            gtags[g] = dma(
                                CG[g % 2],
                                slg[g % 2][:, :],
                                wq1s.ap()[g * 2:(g + 1) * 2, :, :]
                                    .rearrange("a p b -> p a b"),
                                wait=[(w1st["cq"][0].sem, c),
                                      (w1st["cq"][1].sem, c),
                                      (Spe, Spe.c)] +
                                     (callfence if g < 2 and q == 0 else []))
                        if drain:
                            drain(10)
                        load1(0)
                        load1(1)
                        for hc in range(HC):
                            if drain:
                                drain(4)
                            if not cw2_done[0]:
                                tsc("vector", cw2_sb[:, :], cw_sb[:, :],
                                    q1_col[:, 0:1], None, ALU.mult,
                                    wait=[w1st["qw"]], inc=Sdv)
                                cw2_done[0] = True
                            g, b = hc // 2, hc % 2
                            slab = slg[g % 2]
                            soff = (hc % 2) * DC * 128
                            first_w = [gtags[g], (Ssc, sc_done[b]),
                                       (Sdv, dv_done[b])]
                            if hc < 2:
                                first_w += ([(Sdv, q0_fence)] + callfence
                                            if q == 0 else [(Sdv, Sdv.c)])
                            for dc in range(DC):
                                wmm = first_w if dc == 0 else ()
                                if q == 0 and hc == 0:
                                    wmm = list(wmm) + [xsc_tag[dc]]
                                mm(H_ps[b][:, 0:512],
                                   slab[:, soff + dc * 128:
                                        soff + (dc + 1) * 128],
                                   xsc1[:, dc * T + qoff:dc * T + qoff + 512],
                                   start=(dc == 0), stop=(dc == DC - 1),
                                   wait=wmm,
                                   inc=Spe if dc == DC - 1 else None)
                            if q < NQ - 1:
                                for dc in range(DC):
                                    mm(H_ps[b][:, 513:514],
                                       slab[:, soff + dc * 128:
                                            soff + (dc + 1) * 128],
                                       xsc1[:, dc * T + qoff + 512:
                                            dc * T + qoff + 513],
                                       start=(dc == 0), stop=(dc == DC - 1),
                                       inc=Spe if dc == DC - 1 else None)
                            pe_tag = Spe.c
                            if hc % 2 == 1 and g + 2 < 16:
                                load1(g + 2)
                            if hc == HC - 1:
                                slab_cons[0] = slab_cons[1] = pe_tag
                            if q == NQ - 1:
                                memset("vector", H_ps[b][:, 513:514], 0.0,
                                       wait=[(Spe, pe_tag)], inc=Sdv)
                            cp("vector", Hl[(q + 1) % 2][:, hc:hc + 1],
                               H_ps[b][:, 511:512],
                               wait=[(Spe, pe_tag)], inc=Sdv)
                            act(atmp[:, :], H_ps[b][:, 0:512], AF.Identity,
                                bias=cb_sb[:, hc:hc + 1],
                                scale=cw2_sb[:, 3 * hc + 1:3 * hc + 2],
                                wait=[(Spe, pe_tag), (Sdv, Sdv.c)], inc=Ssc)
                            stt("vector", t1tmp[:, 0:511], H_ps[b][:, 1:512],
                                cw2_sb[:, 3 * hc + 2:3 * hc + 3],
                                atmp[:, 0:511], ALU.mult, ALU.add,
                                wait=[(Ssc, Ssc.c)], inc=Sdv)
                            stt("vector", t1tmp[:, 511:512],
                                H_ps[b][:, 513:514],
                                cw2_sb[:, 3 * hc + 2:3 * hc + 3],
                                atmp[:, 511:512], ALU.mult, ALU.add, inc=Sdv)
                            stt("vector", yconv[:, 1:512], H_ps[b][:, 0:511],
                                cw2_sb[:, 3 * hc:3 * hc + 1], t1tmp[:, 1:512],
                                ALU.mult, ALU.add, inc=Sdv)
                            stt("vector", yconv[:, 0:1],
                                Hl[q % 2][:, hc:hc + 1],
                                cw2_sb[:, 3 * hc:3 * hc + 1], t1tmp[:, 0:1],
                                ALU.mult, ALU.add, inc=Sdv)
                            dv_done[b] = Sdv.c
                            act(ysilu[:, hc * QT:(hc + 1) * QT], yconv[:, :],
                                AF.Silu, wait=[(Sdv, Sdv.c)], inc=Ssc)
                            sc_done[b] = Ssc.c
                            if hc == HC // 2 - 1:
                                red("vector", amax2p[:, :],
                                    bass.AP(ysilu, 0,
                                            [[HC * QT, 128], [1, QT],
                                             [QT, HC // 2]]),
                                    ALU.max, absval=True,
                                    wait=[(Ssc, Ssc.c)], inc=Sdv)
                    if DBG == "ysilu0" and q == 0:
                        dma(CDN, dbg_t.ap()[:, 0:HC * QT], ysilu[:, :],
                            wait=[(Ssc, Ssc.c), (Spe, Spe.c)])
                    red("vector", s2_bc[:, :],
                        bass.AP(ysilu, (HC // 2) * QT,
                                [[HC * QT, 128], [1, QT], [QT, HC // 2]]),
                        ALU.max, absval=True,
                        wait=[(Ssc, Ssc.c), (Spe, Spe.c)], inc=Sdv)
                    tt("vector", amax2p[:, :], amax2p[:, :], s2_bc[:, :],
                       ALU.max, inc=Sdv)
                    with nc.psum_tensor(uid("am2"), [128, 512], F32) as am_ps:
                        for i in range(QT // 128):
                            transpose(am_ps[:, 0:128],
                                      amax2p[:, i * 128:(i + 1) * 128],
                                      ident_f[:, :], wait=[(Sdv, Sdv.c)],
                                      inc=Spe)
                            red("vector", tok_b[:, 60 + i:61 + i],
                                am_ps[:, 0:128], ALU.max,
                                wait=[(Spe, Spe.c)], inc=Sdv)
                    am2 = tok_b[:, 60:64]
                    s2t = tok_b[:, 56:60]
                    sv2t = tok_b[:, 52:56]
                    tsc("vector", am2, am2, 1e-5, None, ALU.max, inc=Sdv)
                    recip("vector", s2t, am2, inc=Sdv)
                    tsc("vector", s2t, s2t, 127.0, None, ALU.mult, inc=Sdv)
                    tsc("vector", sv2t, am2, 0.5 / 127.0, None,
                        ALU.mult, inc=Sdv)
                    bcast_tok(s2t, 4, s2_bc[:, :])
                    b2 = bcast_tok(sv2t, 4, sv2_bc[:, :])
                    with ExitStack() as octx:
                        O_ps = [octx.enter_context(nc.psum_tensor(
                            uid("Ops"), [128, 512], F32)) for _ in range(DC)]
                        g2tags = [None] * 16
                        def load2(g):
                            c = 16 * (2 * g + 2)
                            g2tags[g] = dma(
                                CG[g % 2],
                                slg[g % 2][:, :],
                                wq2s.ap()[g * 2:(g + 1) * 2, :, :]
                                    .rearrange("a p b -> p a b"),
                                wait=[(w2st["cq"][0].sem, c),
                                      (w2st["cq"][1].sem, c),
                                      (Spe, Spe.c)])
                        load2(0)
                        load2(1)
                        o_pe = [0, 0]
                        for hc in range(HC):
                            g, b = hc // 2, hc % 2
                            slab = slg[g % 2]
                            soff = (hc % 2) * DC * 128
                            tt("vector", xi2[:, :],
                               ysilu[:, hc * QT:(hc + 1) * QT], s2_bc[:, :],
                               ALU.mult,
                               wait=[b2, (Spe, o_pe[b])], inc=Sdv)
                            tt("vector", xsc2[b][:, :], xi2[:, :],
                               sv2_bc[:, :], ALU.mult, inc=Sdv)
                            dv_tag = Sdv.c
                            for dc in range(DC):
                                wmm = []
                                if dc == 0:
                                    wmm = [(Sdv, dv_tag), g2tags[g]]
                                    if hc == 0:
                                        wmm += [(Sdv, Sdv.c),
                                                (Ssc, Ssc.c)]
                                mm(O_ps[dc][:, :],
                                   slab[:, soff + dc * 128:
                                        soff + (dc + 1) * 128],
                                   xsc2[b][:, :],
                                   start=(hc == 0), stop=(hc == HC - 1),
                                   wait=wmm, inc=Spe)
                            o_pe[b] = Spe.c
                            if hc % 2 == 1 and g + 2 < 16:
                                load2(g + 2)
                            if hc == HC - 1:
                                slab_cons[0] = slab_cons[1] = Spe.c
                        pe_tag = Spe.c
                        for dc in range(DC):
                            stt("vector",
                                x_sb[:, dc * T + qoff:dc * T + qoff + QT],
                                O_ps[dc][:, :], q2_col[:, 0:1],
                                x_sb[:, dc * T + qoff:dc * T + qoff + QT],
                                ALU.mult, ALU.add,
                                wait=[(Spe, pe_tag), w2st["qw"]], inc=Sdv)
                    if DBG == "xafter0" and q == 0:
                        dma(CDN, dbg_t.ap()[:, 0:DC * T], x_sb[:, :],
                            wait=[(Sdv, Sdv.c)])
                    if DBG == "xsc20" and q == 0:
                        pass

        def attn_l0():
            with ExitStack() as actx:
                asb = lambda n, s, d: actx.enter_context(
                    nc.sbuf_tensor(uid(n), s, d))
                efence = guard()
                xln = asb("axln", [128, DC * T], BF16)
                wv_bf = asb("awv", [128, DC * 512], BF16)
                chv = wchain()

                def load_wv(vh, w0):
                    t = None
                    for dc in range(DC):
                        t = dma(chv, wv_bf[:, dc * 512:(dc + 1) * 512],
                                wqkvT.ap()[dc * 128:(dc + 1) * 128,
                                           2 * D + vh * 512:
                                           2 * D + (vh + 1) * 512],
                                wait=(w0 if dc == 0 else ()))
                    return t
                wv_tags = [load_wv(0, efence + [(Spe, Spe.c)]), None]
                lnr = emit_ln(xln)
                V_tok = asb("avtok", [128, NT * NH * 65], BF16)
                attn_o = asb("aout", [128, DC * T], BF16)
                if True:
                    memset("vector",
                           bass.AP(V_tok, 64,
                                   [[NT * NH * 65, 128], [65, NT * NH]]),
                           1.0, inc=Sdv)
                    if True:
                        for vh in range(2):
                            if vh == 1:
                                wv_tags[1] = load_wv(1, [(Spe, Spe.c)])
                            wv_tag = wv_tags[vh]
                            wvo = 0
                            with nc.psum_tensor(uid("avps"), [128, 1024],
                                                F32) as vps:
                                ev_by = {0: Sdv.c, 1: Sdv.c}
                                for ti in range(NT):
                                    pw = ti % 2
                                    for dc in range(DC):
                                        wmm = []
                                        if dc == 0:
                                            wmm = [wv_tag,
                                                   (Sdv, ev_by[pw])]
                                        if ti == 0 and vh == 0:
                                            wmm = list(wmm) + \
                                                [lnr["xln"][dc]]
                                        mm(vps[:, pw * 512:(pw + 1) * 512],
                                           xln[:, dc * T + ti * 128:
                                               dc * T + (ti + 1) * 128],
                                           wv_bf[:, wvo + dc * 512:
                                                 wvo + (dc + 1) * 512],
                                           start=(dc == 0),
                                           stop=(dc == DC - 1),
                                           wait=wmm,
                                           inc=Spe if dc == DC - 1 else None)
                                    cp("vector",
                                       bass.AP(V_tok,
                                               ti * NH * 65 + vh * 8 * 65,
                                               [[NT * NH * 65, 128], [65, 8],
                                                [1, HD]]),
                                       bass.AP(vps, pw * 512,
                                               [[1024, 128], [64, 8],
                                                [1, HD]]),
                                       wait=[(Spe, Spe.c)], inc=Sdv)
                                    ev_by[pw] = Sdv.c
                    pjctx = ExitStack()
                    psb2 = lambda n, s, d: pjctx.enter_context(
                        nc.sbuf_tensor(uid(n), s, d))
                    den = psb2("aden", [128, T], BF16)
                    E_sb = psb2("aE", [128, 1024], BF16)
                    E_sb2 = psb2("aE2", [128, 1024], BF16)
                    Qp = psb2("aQp", [128, T], BF16)
                    Kp = psb2("aKp", [128, T], BF16)
                    K65 = psb2("aK65", [128, T], BF16)
                    Q65 = psb2("aQ65", [128, T], BF16)
                    d2t = psb2("ad2", [128, 64], F32)
                    acc_col = psb2("aacc", [128, 8], F32)
                    qk_sl = [psb2("aqk", [128, DC * 256], BF16)
                             for _ in range(2)]
                    memset("vector", K65[64:65, :], 1.0, inc=Sdv)
                    qk_tags = [None, None]

                    def load_qk(pj):
                        b = pj % 2
                        dma(CG[b], qk_sl[b][:, 0:DC * 128].rearrange(
                                "p (a b) -> p a b", a=DC),
                            wqkvT.ap()[:, pj * 128:(pj + 1) * 128]
                                .rearrange("(a p) b -> p a b", p=128),
                            wait=[(Spe, Spe.c)] +
                                 (guard() if pj < 2 else []))
                        qk_tags[b] = dma(
                            CG[b],
                            qk_sl[b][:, DC * 128:DC * 256].rearrange(
                                "p (a b) -> p a b", a=DC),
                            wqkvT.ap()[:, D + pj * 128:
                                       D + (pj + 1) * 128]
                                .rearrange("(a p) b -> p a b", p=128))
                    load_qk(0)
                    for pj in range(NH // 2):
                        b = pj % 2
                        qk_tag = qk_tags[b]
                        with nc.psum_tensor(uid("aqkps"), [128, 1024], F32) \
                                as qps:
                            qk_ev = {0: Sdv.c, 1: Sdv.c}
                            for (dst, col0) in ((Qp, 0), (Kp, DC * 128)):
                                for w in range(4):
                                    pw = w % 2
                                    for dc in range(DC):
                                        wmm = [qk_tag,
                                               (Sdv, qk_ev[pw])] \
                                            if dc == 0 else ()
                                        mm(qps[:, pw * 512:(pw + 1) * 512],
                                           qk_sl[b][:, col0 + dc * 128:
                                                   col0 + (dc + 1) * 128],
                                           xln[:, dc * T + w * 512:
                                               dc * T + (w + 1) * 512],
                                           start=(dc == 0),
                                           stop=(dc == DC - 1),
                                           wait=wmm,
                                           inc=Spe if dc == DC - 1 else None)
                                    cp("vector",
                                       dst[:, w * 512:(w + 1) * 512],
                                       qps[:, pw * 512:(pw + 1) * 512],
                                       wait=[(Spe, Spe.c)], inc=Sdv)
                                    qk_ev[pw] = Sdv.c
                        qkp_dv = Sdv.c
                        if pj + 1 < NH // 2:
                            load_qk(pj + 1)
                        memset("vector", d2t[:, 0:32], 0.0, inc=Sdv)
                        with ExitStack() as actx2:
                            SA = [[actx2.enter_context(nc.psum_tensor(
                                uid("aA"), [128, 1024], F32)) for _ in range(2)]
                                for _ in range(2)]
                            expa = [[0, 0], [0, 0]]
                            acc_done = {}
                            acc_i = [0]
                            for qt in range(NT):
                                for ktg in range(2):
                                    par = ktg
                                    for half in range(2):
                                        for hh in range(2):
                                            base = hh * 64
                                            w0 = [(Ssc, expa[hh][par])]
                                            if qt < 2 and ktg == 0 \
                                                    and half == 0:
                                                w0 += [(Sdv, qkp_dv)]
                                                if pj == 0:
                                                    w0 += guard()
                                            mm(SA[hh][par][:, half * 512:
                                                           (half + 1) * 512],
                                               Qp[base:base + 64,
                                                  qt * 128:(qt + 1) * 128],
                                               Kp[base:base + 64,
                                                  ktg * 1024 + half * 512:
                                                  ktg * 1024 + (half + 1) * 512],
                                               start=True, stop=True,
                                               wait=w0,
                                               tile_position=(base, 0),
                                               inc=Spe)
                                    pe_tag = Spe.c
                                    for hh in range(2):
                                        ai = acc_i[0] % 8
                                        acc_i[0] += 1
                                        act(E_sb[:, :],
                                            SA[hh][par][:, :], AF.Exp,
                                            scale=0.015625,
                                            accum=acc_col[:, ai:ai + 1],
                                            wait=[(Spe, pe_tag),
                                                  (Sdv, acc_done.get(ai, 0))],
                                            inc=Ssc)
                                        expa[hh][par] = Ssc.c
                                        tt("vector",
                                           d2t[:, hh * 16 + qt:hh * 16 + qt + 1],
                                           d2t[:, hh * 16 + qt:hh * 16 + qt + 1],
                                           acc_col[:, ai:ai + 1], ALU.add,
                                           wait=[(Ssc, Ssc.c)], inc=Sdv)
                                        acc_done[ai] = Sdv.c
                        with ExitStack() as sctx:
                            S_ps = [sctx.enter_context(nc.psum_tensor(
                                uid("asps"), [128, 1024], F32))
                                for _ in range(2)]
                            P_ps = [sctx.enter_context(nc.psum_tensor(
                                uid("apps"), [128, 512], F32))
                                for _ in range(2)]
                            pv_by = {0: Sdv.c, 1: Sdv.c}
                            bslot = []
                            for hh in range(2):
                                dcol = d2t[:, 32 + hh * 16:48 + hh * 16]
                                tsc("vector", dcol,
                                    d2t[:, hh * 16:hh * 16 + 16],
                                    1e-30, 1e30, ALU.max, ALU.min, inc=Sdv)
                                act(dcol, dcol, AF.Ln, wait=[(Sdv, Sdv.c)],
                                    inc=Ssc)
                                tsc("vector", dcol, dcol, -64.0, None,
                                    ALU.mult, wait=[(Ssc, Ssc.c)], inc=Sdv)
                                with nc.psum_tensor(uid("abp"), [16, 512],
                                                    F32) as bps:
                                    transpose(bps[0:16, 0:128], dcol, ident_f,
                                              wait=[(Sdv, Sdv.c)], inc=Spe)
                                    cp("vector",
                                       trtmp[0:16, 128 * hh:128 * hh + 128],
                                       bps[0:16, 0:128],
                                       wait=[(Spe, Spe.c)] +
                                            ([rowslot["g1"]] if rowslot["g1"]
                                             else []), inc=Sdv)
                                slot = rowslot["i"] % NROW
                                rowslot["i"] += 1
                                old = rowslot["tags"][slot]
                                g1 = dma(CR, rowbuf.ap()[slot:slot + 1, :]
                                         .rearrange("q (p m) -> q p m", p=16),
                                         trtmp[0:16,
                                               128 * hh:128 * hh + 128],
                                         wait=[(Sdv, Sdv.c), old])
                                rowslot["g1"] = g1
                                bslot.append((slot, g1))
                            for hh in range(2):
                                h = 2 * pj + hh
                                base = hh * 64
                                if hh == 0:
                                    cp("vector", K65[0:64, :], Kp[0:64, :],
                                       wait=[(Spe, Spe.c)], inc=Sdv)
                                    cp("vector", Q65[0:64, :], Qp[0:64, :],
                                       inc=Sdv)
                                    c65 = [(Sdv, Sdv.c)]
                                else:
                                    oc1 = dma(CR, K65[0:64, :], Kp[64:128, :],
                                              wait=[(Sdv, Sdv.c),
                                                    (Spe, Spe.c)])
                                    oc2 = dma(CR, Q65[0:64, :], Qp[64:128, :],
                                              wait=[oc1])
                                    c65 = [oc2]
                                slot, g1 = bslot[hh]
                                g2 = dma(CDN, Q65[64:65, :],
                                         rowbuf.ap()[slot:slot + 1, :],
                                         wait=[g1])
                                rowslot["tags"][slot] = g2
                                exp_by = {0: Ssc.c, 1: Ssc.c}
                                first_extra = c65 + [g2, (Sdv, Sdv.c)]
                                Ebuf = [E_sb, E_sb2]

                                def scores_exp(qb, ktp, extra):
                                    pb2 = ktp % 2
                                    for half in range(2):
                                        kt = 2 * ktp + half
                                        w0 = [(Ssc, exp_by[pb2])]
                                        if extra and half == 0:
                                            w0 = w0 + extra
                                        mm(S_ps[pb2][:, half * 512:
                                                     (half + 1) * 512],
                                           K65[0:65,
                                               kt * 128:(kt + 1) * 128],
                                           Q65[0:65,
                                               qb * 512:(qb + 1) * 512],
                                           start=True, stop=True,
                                           wait=w0, inc=Spe)
                                    act(Ebuf[pb2][:, :], S_ps[pb2][:, :],
                                        AF.Exp, scale=0.125,
                                        wait=[(Spe, Spe.c)], inc=Ssc)
                                    exp_by[pb2] = Ssc.c

                                def pv_mms(qb, ktp):
                                    pb2 = ktp % 2
                                    pq = qb % 2
                                    for half in range(2):
                                        kt = 2 * ktp + half
                                        w0 = [(Ssc, exp_by[pb2])]
                                        if kt == 0:
                                            w0.append((Sdv, pv_by[pq]))
                                        mm(P_ps[pq][0:65, 0:512],
                                           V_tok[:, (kt * NH + h) * 65:
                                                 (kt * NH + h) * 65 + 65],
                                           Ebuf[pb2][:, half * 512:
                                                      (half + 1) * 512],
                                           start=(kt == 0),
                                           stop=(kt == NT - 1),
                                           wait=w0, inc=Spe)

                                seq = [(qb, ktp) for qb in range(4)
                                       for ktp in range(NT // 2)]
                                scores_exp(seq[0][0], seq[0][1], first_extra)
                                for i, (qb, ktp) in enumerate(seq):
                                    if i + 1 < len(seq):
                                        scores_exp(seq[i + 1][0],
                                                   seq[i + 1][1], None)
                                    pv_mms(qb, ktp)
                                    if ktp == NT // 2 - 1:
                                        pq = qb % 2
                                        pe_tag = Spe.c
                                        cp("vector",
                                           den[64:65,
                                               qb * 512:(qb + 1) * 512],
                                           P_ps[pq][64:65, 0:512],
                                           wait=[(Spe, pe_tag)], inc=Sdv)
                                        cp("vector",
                                           attn_o[base:base + 64,
                                                  pj * T + qb * 512:
                                                  pj * T + (qb + 1) * 512],
                                           P_ps[pq][0:64, 0:512], inc=Sdv)
                                        pv_by[pq] = Sdv.c
                                recip("vector", den[64:65, :],
                                      den[64:65, :], inc=Sdv)
                                slot = rowslot["i"] % NROW
                                rowslot["i"] += 1
                                old = rowslot["tags"][slot]
                                g1 = dma(CDN, rowbuf.ap()[slot:slot + 1, :],
                                         den[64:65, :],
                                         wait=[(Sdv, Sdv.c), old])
                                g2 = dma(CDN, den[base:base + 64, :],
                                         bass.AP(rowbuf, slot * T,
                                                 [[0, 64], [1, T]]),
                                         wait=[g1])
                                rowslot["tags"][slot] = g2
                                tt("vector",
                                   attn_o[base:base + 64,
                                          pj * T:(pj + 1) * T],
                                   attn_o[base:base + 64,
                                          pj * T:(pj + 1) * T],
                                   den[base:base + 64, :], ALU.mult,
                                   wait=[g2], inc=Sdv)
                    pjctx.close()
                    wo_bf = asb("awo", [128, DC * D], BF16)
                    ch = wchain()
                    for ec in range(DC):
                        wo_tag = dma(ch, wo_bf[:, ec * D:(ec + 1) * D],
                                     woT.ap()[ec * 128:(ec + 1) * 128, :],
                                     wait=(guard() if ec == 0 else ()))
                if True:
                    with nc.psum_tensor(uid("aops"), [128, 1024], F32) as ops:
                        first = True
                        ev_by = {0: Sdv.c, 1: Sdv.c}
                        for dc in range(DC):
                            for qb in range(4):
                                pw = (dc * 4 + qb) % 2
                                for ec in range(DC):
                                    wmm = []
                                    if ec == 0:
                                        wmm = [wo_tag, (Sdv, ev_by[pw])]
                                        if first:
                                            wmm += guard()
                                            first = False
                                    mm(ops[:, pw * 512:(pw + 1) * 512],
                                       wo_bf[:, ec * D + dc * 128:
                                             ec * D + (dc + 1) * 128],
                                       attn_o[:, ec * T + qb * 512:
                                              ec * T + (qb + 1) * 512],
                                       start=(ec == 0), stop=(ec == DC - 1),
                                       wait=wmm,
                                       inc=Spe if ec == DC - 1 else None)
                                stt("vector",
                                    x_sb[:, dc * T + qb * 512:
                                         dc * T + (qb + 1) * 512],
                                    ops[:, pw * 512:(pw + 1) * 512],
                                    bo_sb[:, dc:dc + 1],
                                    x_sb[:, dc * T + qb * 512:
                                         dc * T + (qb + 1) * 512],
                                    ALU.add, ALU.add,
                                    wait=[(Spe, Spe.c)], inc=Sdv)
                                ev_by[pw] = Sdv.c

        def moe_l1():
            with ExitStack() as mctx:
                msb = lambda n, s, d: mctx.enter_context(
                    nc.sbuf_tensor(uid(n), s, d))
                efence = guard()
                xln = msb("mxln", [128, DC * T], BF16)
                rm_keep = msb("mrm", [128, 32], F32)
                mrow = msb("mrow", [1, T], F32)
                lnr = emit_ln(xln, keep=rm_keep)
                hsum = msb("mh", [128, DC * T], BF16)
                s0_bc = msb("ms0", [128, T], F32)
                s1_bc = msb("ms1", [128, T], F32)
                gate_sb = msb("mgt", [8, T], F32)
                gtok = msb("mgtok", [128, 128], F32)
                etmp = msb("met", [128, 512], BF16)
                with nc.psum_tensor(uid("mmtp"), [16, 512], F32) as mtp:
                    transpose(mtp[0:16, 0:128], rm_keep[:, 16:32], ident_f,
                              wait=[(Sdv, lnr["keep"]), lnr["b2"]], inc=Spe)
                    cp("vector", trtmp[0:16, 0:128], mtp[0:16, 0:128],
                       wait=[(Spe, Spe.c)] +
                            ([rowslot["g1"]] if rowslot["g1"] else []),
                       inc=Sdv)
                slot = rowslot["i"] % NROW
                rowslot["i"] += 1
                old_t = rowslot["tags"][slot]
                g1 = dma(CR, rowbuf.ap()[slot:slot + 1, :]
                         .rearrange("q (p m) -> q p m", p=16),
                         trtmp[0:16, 0:128], wait=[(Sdv, Sdv.c), old_t])
                rowslot["g1"] = g1
                g2 = dma(CR, mrow[0:1, :], rowbuf.ap()[slot:slot + 1, :],
                         wait=[g1])
                rowslot["tags"][slot] = g2
                with ExitStack() as gctx:
                    gw_f = gctx.enter_context(
                        nc.sbuf_tensor(uid("mgw"), [128, DC * NE + NE], F32))
                    for dc in range(DC):
                        gw_tag = dma(CR, gw_f[:, dc * NE:(dc + 1) * NE],
                                     gwT.ap()[dc * 128:(dc + 1) * 128, :],
                                     wait=(efence + [(Spe, Spe.c)]
                                           if dc == 0 else ()))
                    with nc.psum_tensor(uid("mgsum"), [1, 512], F32) as gsps:
                        for dc in range(DC):
                            mm(gsps[0:1, 0:NE], ones_f[:, :],
                               gw_f[:, dc * NE:(dc + 1) * NE],
                               start=(dc == 0), stop=(dc == DC - 1),
                               wait=[gw_tag] if dc == 0 else (),
                               inc=Spe if dc == DC - 1 else None)
                        tsc("vector", gw_f[0:1, DC * NE:DC * NE + NE],
                            gsps[0:1, 0:NE], -1.0, None, ALU.mult,
                            wait=[(Spe, Spe.c)], inc=Sdv)
                        gsum_dv = (Sdv, Sdv.c)
                    with nc.psum_tensor(uid("mgps"), [8, T], F32) as gps:
                        for w in range(4):
                            for dc in range(DC):
                                wmm = ([gw_tag, lnr["b2"],
                                        (Sdv, Sdv.c)] + efence) \
                                    if (dc == 0 and w == 0) else ()
                                mm(gps[0:NE, w * 512:(w + 1) * 512],
                                   gw_f[:, dc * NE:(dc + 1) * NE],
                                   x_sb[:, dc * T + w * 512:
                                        dc * T + (w + 1) * 512],
                                   start=(dc == 0), stop=False,
                                   wait=wmm)
                            mm(gps[0:NE, w * 512:(w + 1) * 512],
                               gw_f[0:1, DC * NE:DC * NE + NE],
                               mrow[0:1, w * 512:(w + 1) * 512],
                               start=False, stop=True,
                               wait=[(Sdv, Sdv.c), g2] if w == 0 else (),
                               inc=Spe)
                        cp("vector", gate_sb[0:8, :], gps[0:8, :],
                           wait=[(Spe, Spe.c)], inc=Sdv)
                with nc.psum_tensor(uid("mgtp"), [128, 512], F32) as gtp:
                    for i in range(NT):
                        transpose(gtp[:, i * 8:(i + 1) * 8],
                                  gate_sb[0:8, i * 128:(i + 1) * 128],
                                  ident_f,
                                  wait=[(Sdv, Sdv.c)] if i == 0 else (),
                                  inc=Spe)
                    cp("vector", gtok[:, :], gtp[:, 0:128],
                       wait=[(Spe, Spe.c)], inc=Sdv)
                for i in range(NT):
                    tsc("vector", gtok[:, i * 8:(i + 1) * 8],
                        gtok[:, i * 8:(i + 1) * 8],
                        rm_keep[:, i:i + 1], None, ALU.mult, inc=Sdv)
                gmax = tok_b[:, 0:16]
                gsum = tok_b[:, 16:32]
                red("vector", gmax,
                    bass.AP(gtok, 0, [[128, 128], [8, NT], [1, 8]]),
                    ALU.max, inc=Sdv)
                tsc("vector", gmax, gmax, -1.0, None, ALU.mult, inc=Sdv)
                for i in range(NT):
                    act(gtok[:, i * 8:(i + 1) * 8],
                        gtok[:, i * 8:(i + 1) * 8],
                        AF.Exp, bias=gmax[:, i:i + 1],
                        wait=[(Sdv, Sdv.c)] if i == 0 else (), inc=Ssc)
                red("vector", gsum,
                    bass.AP(gtok, 0, [[128, 128], [8, NT], [1, 8]]),
                    ALU.add, wait=[(Ssc, Ssc.c)], inc=Sdv)
                thr = tok_b[:, 32:48]
                c1 = tok_b[:, 48:64]
                tsc("vector", thr, gsum, 0.3, None, ALU.mult, inc=Sdv)
                for i in range(NT):
                    tsc("vector", tok_a[:, 0:8],
                        gtok[:, i * 8:(i + 1) * 8], thr[:, i:i + 1], None,
                        ALU.is_gt, inc=Sdv)
                    red("vector", c1[:, i:i + 1], tok_a[:, 0:8], ALU.add,
                        inc=Sdv)
                recip("vector", gsum, gsum, inc=Sdv)
                p0 = tok_a[:, 0:16]
                p1 = tok_a[:, 16:32]
                tt("vector", p0, bass.AP(gtok, 0, [[128, 128], [8, NT]]),
                   gsum, ALU.mult, inc=Sdv)
                tt("vector", p1, bass.AP(gtok, 1, [[128, 128], [8, NT]]),
                   gsum, ALU.mult, inc=Sdv)
                s0 = tok_a[:, 32:48]
                s1 = tok_a[:, 48:64]
                tsc("vector", s0, c1, -0.5, 4.0, ALU.mult, ALU.add, inc=Sdv)
                tt("vector", s0, s0, p0, ALU.mult, inc=Sdv)
                tsc("vector", s1, c1, 0.5, None, ALU.mult, inc=Sdv)
                tt("vector", s1, s1, p1, ALU.mult, inc=Sdv)
                bcast_tok(s0, NT, s0_bc[:, :])
                bs1 = bcast_tok(s1, NT, s1_bc[:, :])
                for ei, (ew_in, sbc) in enumerate(((e0T, s0_bc),
                                                   (e1T, s1_bc))):
                    with ExitStack() as ectx:
                        ew_bf = ectx.enter_context(
                            nc.sbuf_tensor(uid("mew"), [128, DC * D], BF16))
                        ch = wchain()
                        for dc in range(DC):
                            ew_tag = dma(ch, ew_bf[:, dc * D:(dc + 1) * D],
                                         ew_in.ap()[dc * 128:(dc + 1) * 128, :],
                                         wait=(efence +
                                               [(Spe, Spe.c), gsum_dv]
                                               if dc == 0 else ()))
                        with nc.psum_tensor(uid("meps"), [128, 1024], F32) \
                                as eps:
                            ev_by = {0: Sdv.c, 1: Sdv.c}
                            for hc in range(DC):
                                for qb in range(4):
                                    pw = (hc * 4 + qb) % 2
                                    for dc in range(DC):
                                        wmm = []
                                        if dc == 0:
                                            wmm = [ew_tag,
                                                   (Sdv, ev_by[pw])]
                                        mm(eps[:, pw * 512:(pw + 1) * 512],
                                           ew_bf[:, dc * D + hc * 128:
                                                 dc * D + (hc + 1) * 128],
                                           xln[:, dc * T + qb * 512:
                                               dc * T + (qb + 1) * 512],
                                           start=(dc == 0),
                                           stop=(dc == DC - 1),
                                           wait=wmm,
                                           inc=Spe if dc == DC - 1 else None)
                                    dst = hsum[:, hc * T + qb * 512:
                                               hc * T + (qb + 1) * 512]
                                    w0 = [(Spe, Spe.c)]
                                    if hc == 0 and qb == 0:
                                        w0.append(bs1)
                                    if ei == 0:
                                        tt("vector", dst,
                                           eps[:, pw * 512:(pw + 1) * 512],
                                           sbc[:, qb * 512:(qb + 1) * 512],
                                           ALU.mult, wait=w0, inc=Sdv)
                                    else:
                                        tt("vector", etmp[:, :],
                                           eps[:, pw * 512:(pw + 1) * 512],
                                           sbc[:, qb * 512:(qb + 1) * 512],
                                           ALU.mult, wait=w0, inc=Sdv)
                                        tt("vector", dst, dst, etmp[:, :],
                                           ALU.add, inc=Sdv)
                                    ev_by[pw] = Sdv.c
                with ExitStack() as wctx:
                    ow_bf = wctx.enter_context(
                        nc.sbuf_tensor(uid("mow"), [128, DC * D], BF16))
                    ch = wchain()
                    for hc in range(DC):
                        ow_tag = dma(ch, ow_bf[:, hc * D:(hc + 1) * D],
                                     owT.ap()[hc * 128:(hc + 1) * 128, :],
                                     wait=(efence + [(Spe, Spe.c)]
                                           if hc == 0 else ()))
                    with nc.psum_tensor(uid("mops"), [128, 1024], F32) as ops:
                        first = True
                        ev_by = {0: Sdv.c, 1: Sdv.c}
                        for dc in range(DC):
                            for qb in range(4):
                                pw = (dc * 4 + qb) % 2
                                for hc in range(DC):
                                    wmm = []
                                    if hc == 0:
                                        wmm = [ow_tag, (Sdv, ev_by[pw])]
                                        if first:
                                            wmm += guard()
                                            first = False
                                    mm(ops[:, pw * 512:(pw + 1) * 512],
                                       ow_bf[:, hc * D + dc * 128:
                                             hc * D + (dc + 1) * 128],
                                       hsum[:, hc * T + qb * 512:
                                            hc * T + (qb + 1) * 512],
                                       start=(hc == 0), stop=(hc == DC - 1),
                                       wait=wmm,
                                       inc=Spe if hc == DC - 1 else None)
                                tt("vector",
                                   x_sb[:, dc * T + qb * 512:
                                        dc * T + (qb + 1) * 512],
                                   ops[:, pw * 512:(pw + 1) * 512],
                                   x_sb[:, dc * T + qb * 512:
                                        dc * T + (qb + 1) * 512],
                                   ALU.add, wait=[(Spe, Spe.c)], inc=Sdv)
                                ev_by[pw] = Sdv.c

        def attn_l1():
            with ExitStack() as actx:
                asb = lambda n, s, d: actx.enter_context(
                    nc.sbuf_tensor(uid(n), s, d))
                efence = guard()
                xln = asb("lxln", [128, DC * T], BF16)
                emit_ln(xln)
                qT = asb("lq", [128, KVC * T], BF16)
                kT = asb("lk", [128, KVC * T], BF16)
                Ktok = asb("lktok", [128, NT * KV], BF16)
                Vtok = asb("lvtok", [128, NT * KV], BF16)
                yT = asb("ly", [128, KVC * T], BF16)
                S_sb = asb("ls", [128, KVC * KV], F32)
                S_bf = asb("lsbf", [128, KVC * KV], BF16)
                Am = asb("lam", [128, 128], BF16)
                wbuf = asb("lw", [128, DC * KV], BF16)
                if True:
                    def proj_fm(w_in, dst):
                        ch = wchain()
                        for dc in range(DC):
                            wt = dma(ch, wbuf[:, dc * KV:(dc + 1) * KV],
                                     w_in.ap()[dc * 128:(dc + 1) * 128, :],
                                     wait=(efence + [(Spe, Spe.c)]
                                           if dc == 0 else ()))
                        with nc.psum_tensor(uid("lpps"), [128, 1024], F32) \
                                as pps:
                            first = True
                            pev = {0: Sdv.c, 1: Sdv.c}
                            for kc in range(KVC):
                                for w in range(4):
                                    pw = (kc * 4 + w) % 2
                                    for dc in range(DC):
                                        wmm = []
                                        if dc == 0:
                                            wmm = [wt, (Sdv, pev[pw])]
                                            if first:
                                                wmm += guard()
                                                first = False
                                        mm(pps[:, pw * 512:(pw + 1) * 512],
                                           wbuf[:, dc * KV + kc * 128:
                                                dc * KV + (kc + 1) * 128],
                                           xln[:, dc * T + w * 512:
                                               dc * T + (w + 1) * 512],
                                           start=(dc == 0),
                                           stop=(dc == DC - 1),
                                           wait=wmm,
                                           inc=Spe if dc == DC - 1 else None)
                                    cp("vector",
                                       dst[:, kc * T + w * 512:
                                           kc * T + (w + 1) * 512],
                                       pps[:, pw * 512:(pw + 1) * 512],
                                       wait=[(Spe, Spe.c)], inc=Sdv)
                                    pev[pw] = Sdv.c

                    def proj_tok(w_in, dst):
                        ch = wchain()
                        for dc in range(DC):
                            wt = dma(ch, wbuf[:, dc * KV:(dc + 1) * KV],
                                     w_in.ap()[dc * 128:(dc + 1) * 128, :],
                                     wait=(efence + [(Spe, Spe.c)]
                                           if dc == 0 else ()))
                        with nc.psum_tensor(uid("ltps"), [128, 1024], F32) \
                                as tps:
                            first = True
                            pev = {0: Sdv.c, 1: Sdv.c}
                            for ti in range(NT):
                                pw = ti % 2
                                for dc in range(DC):
                                    wmm = []
                                    if dc == 0:
                                        wmm = [wt, (Sdv, pev[pw])]
                                        if first:
                                            wmm += guard()
                                            first = False
                                    mm(tps[:, pw * 512:(pw + 1) * 512],
                                       xln[:, dc * T + ti * 128:
                                           dc * T + (ti + 1) * 128],
                                       wbuf[:, dc * KV:(dc + 1) * KV],
                                       start=(dc == 0), stop=(dc == DC - 1),
                                       wait=wmm,
                                       inc=Spe if dc == DC - 1 else None)
                                cp("vector", dst[:, ti * KV:(ti + 1) * KV],
                                   tps[:, pw * 512:(pw + 1) * 512],
                                   wait=[(Spe, Spe.c)], inc=Sdv)
                                pev[pw] = Sdv.c

                    proj_fm(wqT1, qT)
                    proj_fm(wkT1, kT)
                    proj_tok(wkT1, Ktok)
                    proj_tok(wvT1, Vtok)
                with ExitStack() as sctx:
                    S_ps = [sctx.enter_context(nc.psum_tensor(
                        uid("lsps"), [128, 512], F32)) for _ in range(KVC)]
                    A_ps = [sctx.enter_context(nc.psum_tensor(
                        uid("laps"), [128, 512], F32)) for _ in range(2)]
                    y_ps = [sctx.enter_context(nc.psum_tensor(
                        uid("lyps"), [128, 512], F32)) for _ in range(2)]
                    memset("vector", S_sb[:, :], 0.0, wait=guard(),
                           inc=Sdv)
                    ydone = [Sdv.c, Sdv.c]
                    amdone = [Sdv.c, Sdv.c]
                    sevac_lc = {lc: Sdv.c for lc in range(KVC)}
                    for c in range(NT):
                        pb = c % 2
                        for kc in range(KVC):
                            wmm = []
                            if kc == 0:
                                wmm = [(Sdv, amdone[pb])]
                                if c == 0:
                                    wmm += guard()
                            mm(A_ps[pb][:, 0:128],
                               kT[:, kc * T + c * 128:kc * T + (c + 1) * 128],
                               qT[:, kc * T + c * 128:kc * T + (c + 1) * 128],
                               start=(kc == 0), stop=(kc == KVC - 1),
                               wait=wmm, inc=Spe if kc == KVC - 1 else None)
                        tt("vector", Am[:, :], A_ps[pb][:, 0:128], mask_su[:, :],
                           ALU.mult, wait=[(Spe, Spe.c)], inc=Sdv)
                        amdone[pb] = Sdv.c
                        am_tag = Sdv.c
                        for jc in range(KVC):
                            last = (c == 0)
                            mm(y_ps[pb][:, jc * 128:(jc + 1) * 128],
                               Vtok[:, c * KV + jc * 128:
                                    c * KV + (jc + 1) * 128],
                               Am[:, :], start=True, stop=last,
                               wait=[(Sdv, am_tag), (Sdv, ydone[pb])]
                               if jc == 0 else (),
                               inc=Spe if last else None)
                            if c > 0:
                                for lc in range(KVC):
                                    mm(y_ps[pb][:, jc * 128:(jc + 1) * 128],
                                       S_bf[:, lc * KV + jc * 128:
                                            lc * KV + (jc + 1) * 128],
                                       qT[:, lc * T + c * 128:
                                          lc * T + (c + 1) * 128],
                                       start=False, stop=(lc == KVC - 1),
                                       wait=[(Sdv, sevac_lc[lc])]
                                       if jc == 0 else (),
                                       inc=Spe if lc == KVC - 1 else None)
                        for lc in range(KVC):
                            mm(S_ps[lc][:, :],
                               Ktok[:, c * KV + lc * 128:
                                    c * KV + (lc + 1) * 128],
                               Vtok[:, c * KV:(c + 1) * KV],
                               start=True, stop=True,
                               wait=[(Sdv, sevac_lc[lc])],
                               inc=Spe)
                        ds_tag = Spe.c
                        cp("vector",
                           bass.AP(yT, c * 128,
                                   [[KVC * T, 128], [T, KVC], [1, 128]]),
                           bass.AP(y_ps[pb], 0,
                                   [[512, 128], [128, KVC], [1, 128]]),
                           wait=[(Spe, ds_tag)], inc=Sdv)
                        ydone[pb] = Sdv.c
                        if c < NT - 1:
                            for lc in range(KVC):
                                tt("vector", S_sb[:, lc * KV:(lc + 1) * KV],
                                   S_sb[:, lc * KV:(lc + 1) * KV],
                                   S_ps[lc][:, :], ALU.add,
                                   wait=[(Spe, ds_tag)] if lc == 0 else (),
                                   inc=Sdv)
                                cp("vector", S_bf[:, lc * KV:(lc + 1) * KV],
                                   S_sb[:, lc * KV:(lc + 1) * KV], inc=Sdv)
                                sevac_lc[lc] = Sdv.c
                with ExitStack() as wctx:
                    wo_bf = wctx.enter_context(
                        nc.sbuf_tensor(uid("lwo"), [128, KVC * D], BF16))
                    ch = wchain()
                    for kc in range(KVC):
                        wo_tag = dma(ch, wo_bf[:, kc * D:(kc + 1) * D],
                                     woT1_in.ap()[kc * 128:(kc + 1) * 128, :],
                                     wait=(efence + [(Spe, Spe.c),
                                                     (Sdv, Sdv.c)]
                                           if kc == 0 else ()))
                    with nc.psum_tensor(uid("lops"), [128, 1024], F32) as ops:
                        first = True
                        ev_by = {0: Sdv.c, 1: Sdv.c}
                        for dc in range(DC):
                            for qb in range(4):
                                pw = qb % 2
                                for kc in range(KVC):
                                    wmm = []
                                    if kc == 0:
                                        wmm = [wo_tag, (Sdv, ev_by[pw])]
                                        if first:
                                            wmm += guard()
                                            first = False
                                    mm(ops[:, pw * 512:(pw + 1) * 512],
                                       wo_bf[:, kc * D + dc * 128:
                                             kc * D + (dc + 1) * 128],
                                       yT[:, kc * T + qb * 512:
                                          kc * T + (qb + 1) * 512],
                                       start=(kc == 0), stop=(kc == KVC - 1),
                                       wait=wmm,
                                       inc=Spe if kc == KVC - 1 else None)
                                tt("vector",
                                   x_sb[:, dc * T + qb * 512:
                                        dc * T + (qb + 1) * 512],
                                   ops[:, pw * 512:(pw + 1) * 512],
                                   x_sb[:, dc * T + qb * 512:
                                        dc * T + (qb + 1) * 512],
                                   ALU.add, wait=[(Spe, Spe.c)], inc=Sdv)
                                ev_by[pw] = Sdv.c

        def ffn_first():
            ffn_l0(prep_pre, prep_drain)
            while w1steps or w2steps:
                prep_drain(99)
            w1st["pctx"].close()
            w2st["pctx"].close()

        subs = [ffn_first, attn_l0, ffn_l0,
                moe_l1, attn_l1, moe_l1]
        for f in subs[:BUILD_UPTO]:
            f()

        for dc in range(DC):
            st_tag = dma(CST, out_t.ap()[dc * 128:(dc + 1) * 128, :],
                         x_sb[:, dc * T:(dc + 1) * T],
                         wait=guard() if dc == 0 else ())
        em.wait_only("sync", [st_tag])
        em.finish()
    return nc


def _host_inputs(inp):
    bf = ml_dtypes.bfloat16
    x = np.asarray(inp["L0_attn_x"])
    C = np.ascontiguousarray
    w1 = np.asarray(inp["L0_ffn_w1"])
    w2 = np.asarray(inp["L0_ffn_w2"])
    conv_w = np.asarray(inp["L0_ffn_conv_w"])
    shared = {
        "wqkvT": C(np.asarray(inp["L0_attn_W_qkv"]).T),
        "woT": C(np.asarray(inp["L0_attn_W_o"]).T),
        "bo": C(np.asarray(inp["L0_attn_b_o"]).reshape(DC, 128).T),
        "w1h": C(w1.T.reshape(DC, 128, HC, 128).transpose(1, 2, 0, 3)),
        "w2h": C(w2.T.reshape(HC, 128, DC, 128).transpose(1, 0, 2, 3)),
        "cw": C(conv_w[:, 0, :].reshape(HC, 128, 3).transpose(1, 0, 2)
                .reshape(128, HC * 3)),
        "cb": C(np.asarray(inp["L0_ffn_conv_b"]).reshape(HC, 128).T),
        "wqT1": C(np.asarray(inp["L1_attn_Wq"]).T),
        "wkT1": C(np.asarray(inp["L1_attn_Wk"]).T),
        "wvT1": C(np.asarray(inp["L1_attn_Wv"]).T),
        "woT1": C(np.asarray(inp["L1_attn_Wo"]).T),
        "gwT": C(np.asarray(inp["L1_ffn_gate_W"]).T),
        "e0T": C(np.asarray(inp["L1_ffn_expert_W"])[0].T),
        "e1T": C(np.asarray(inp["L1_ffn_expert_W"])[1].T),
        "owT": C(np.asarray(inp["L1_ffn_out_W"]).T),
        "ident_bf": np.eye(128, dtype=bf),
        "ident_f": np.eye(128, dtype=np.float32),
        "ones_bf": np.ones((128, 1), dtype=bf),
        "ones_f": np.ones((128, 1), dtype=np.float32),
        "mask_su": np.triu(np.ones((128, 128), dtype=np.float32), 1)
                     .astype(bf),
    }
    maps = []
    for c in range(8):
        m = dict(shared)
        m["x"] = C(x[c].T.astype(np.float32))
        maps.append(m)
    return maps


def kernel(**inputs):
    maps = _host_inputs(inputs)
    nc = build_program()
    res = run_bass_kernel_spmd(nc, maps, core_ids=list(range(8)))
    out = np.stack([np.asarray(r["out"], dtype=np.float32).T
                    for r in res.results], axis=0)
    return out
